# version 03 (final)
# baseline (speedup 1.0000x reference)
import numpy as np
from contextlib import ExitStack
import concourse.bass as bass
import concourse.mybir as mybir
from concourse.bass_utils import run_bass_kernel_spmd

F32 = mybir.dt.float32
AX = mybir.AxisListType.X
OP = mybir.AluOpType
AF = mybir.ActivationFunctionType

NCORE = 8
TOK = 16384
TPC = TOK // NCORE
D = 2048
NT = TPC // 128
KC = D // 128
CPL = 2
NCH = 256
HD = 64
K16 = 16


class P:

    def __init__(self):
        self.items = []

    def dma(self, out, in_):
        self.items.append(("sync", (lambda e, o=out, i=in_: e.dma_start(out=o, in_=i)), 16))

    def t(self, meth, *a, **k):
        self.items.append(("tensor", (lambda e, m=meth, a=a, k=k: getattr(e, m)(*a, **k)), 1))

    def v(self, meth, *a, **k):
        self.items.append(("vector", (lambda e, m=meth, a=a, k=k: getattr(e, m)(*a, **k)), 1))

    def s(self, meth, *a, **k):
        self.items.append(("scalar", (lambda e, m=meth, a=a, k=k: getattr(e, m)(*a, **k)), 1))

    def build(self, nc, sem):
        total = 0
        sched = []
        for eng, fn, inc in self.items:
            sched.append((eng, fn, inc, total))
            total += inc
        with nc.Block() as blk:
            for name in ("sync", "tensor", "vector", "scalar"):
                ops = [x for x in sched if x[0] == name]
                if not ops and name != "vector":
                    continue

                def mk(name=name, ops=ops):
                    def body(eng):
                        for (_, fn, inc, before) in ops:
                            if before:
                                eng.wait_ge(sem, before)
                            ins = fn(eng)
                            ins.then_inc(sem, inc)
                        if name == "vector":
                            eng.wait_ge(sem, total)
                    return body
                getattr(blk, name)(mk())


def build_ln(eps):
    nc = bass.Bass("TRN2", target_bir_lowering=False)
    xin = nc.dram_tensor("xin", [TPC, D], F32, kind="ExternalInput").ap()
    xout = nc.dram_tensor("xout", [TPC, D], F32, kind="ExternalOutput").ap()
    p = P()
    with ExitStack() as st:
        e = st.enter_context
        xt = e(nc.sbuf_tensor("xt", [128, D], F32))
        yt = e(nc.sbuf_tensor("yt", [128, D], F32))
        mus = e(nc.sbuf_tensor("mus", [128, 1], F32))
        mu = e(nc.sbuf_tensor("mu", [128, 1], F32))
        vs = e(nc.sbuf_tensor("vs", [128, 1], F32))
        rstd = e(nc.sbuf_tensor("rstd", [128, 1], F32))
        sem = e(nc.semaphore("S"))
        for t in range(NT):
            r = slice(t * 128, (t + 1) * 128)
            p.dma(xt[:], xin[r, :])
            p.v("tensor_reduce", mus[:], xt[:], AX, OP.add)
            p.v("tensor_scalar_mul", mu[:], mus[:], 1.0 / D)
            p.v("tensor_scalar", yt[:], xt[:], mu[:], None, OP.subtract)
            p.v("tensor_mul", xt[:], yt[:], yt[:])
            p.v("tensor_reduce", vs[:], xt[:], AX, OP.add)
            p.v("tensor_scalar", vs[:], vs[:], 1.0 / D, eps, OP.mult, OP.add)
            p.s("sqrt", rstd[:], vs[:])
            p.v("reciprocal", rstd[:], rstd[:])
            p.v("tensor_scalar", yt[:], yt[:], rstd[:], None, OP.mult)
            p.dma(xout[r, :], yt[:])
        p.build(nc, sem)
    return nc


def build_mm(resid):
    nc = bass.Bass("TRN2", target_bir_lowering=False)
    xTc = nc.dram_tensor("xTc", [128, KC, TPC], F32, kind="ExternalInput").ap()
    wc = nc.dram_tensor("wc", [KC, 128, D], F32, kind="ExternalInput").ap()
    if resid:
        res = nc.dram_tensor("res", [TPC, D], F32, kind="ExternalInput").ap()
    yout = nc.dram_tensor("yout", [TPC, D], F32, kind="ExternalOutput").ap()
    p = P()
    with ExitStack() as st:
        e = st.enter_context
        lt = e(nc.sbuf_tensor("lt", [128, KC, 128], F32))
        wb = e(nc.sbuf_tensor("wb", [128, D], F32))
        ot = e(nc.sbuf_tensor("ot", [128, D], F32))
        rt = e(nc.sbuf_tensor("rt", [128, D], F32)) if resid else None
        ps = e(nc.psum_tensor("ps", [128, D], F32))
        sem = e(nc.semaphore("S"))
        for t in range(NT):
            r = slice(t * 128, (t + 1) * 128)
            p.dma(lt[:], xTc[:, :, r])
            for c in range(KC):
                p.dma(wb[:], wc[c])
                for jg in range(4):
                    js = slice(jg * 512, jg * 512 + 512)
                    p.t("matmul", ps[:, js], lt[:, c, :], wb[:, js],
                        start=(c == 0), stop=(c == KC - 1))
            if resid:
                p.dma(rt[:], res[r, :])
                p.v("tensor_add", ot[:], ps[:], rt[:])
            else:
                p.v("tensor_copy", ot[:], ps[:])
            p.dma(yout[r, :], ot[:])
        p.build(nc, sem)
    return nc


def build_ilr():
    nc = bass.Bass("TRN2", target_bir_lowering=False)
    xTc = nc.dram_tensor("xTc", [128, KC, TPC], F32, kind="ExternalInput").ap()
    wc = nc.dram_tensor("wc", [KC, 128, 32], F32, kind="ExternalInput").ap()
    lbt = nc.dram_tensor("lbt", [128, 32], F32, kind="ExternalInput").ap()
    yout = nc.dram_tensor("yout", [TPC, 32], F32, kind="ExternalOutput").ap()
    p = P()
    with ExitStack() as st:
        e = st.enter_context
        lt = e(nc.sbuf_tensor("lt", [128, KC, 128], F32))
        wb = e(nc.sbuf_tensor("wb", [128, KC, 32], F32))
        lb = e(nc.sbuf_tensor("lb", [128, 32], F32))
        ot = e(nc.sbuf_tensor("ot", [128, 32], F32))
        ps = e(nc.psum_tensor("ps", [128, 32], F32))
        sem = e(nc.semaphore("S"))
        p.dma(wb[:], wc.transpose([1, 0, 2]))
        p.dma(lb[:], lbt[:, :])
        for t in range(NT):
            r = slice(t * 128, (t + 1) * 128)
            p.dma(lt[:], xTc[:, :, r])
            for c in range(KC):
                p.t("matmul", ps[:], lt[:, c, :], wb[:, c, :], start=(c == 0), stop=(c == KC - 1))
            p.v("tensor_add", ot[:], ps[:], lb[:])
            p.s("activation", ot[:], ot[:], AF.Sigmoid)
            p.dma(yout[r, :], ot[:])
        p.build(nc, sem)
    return nc


def build_gate():
    nc = bass.Bass("TRN2", target_bir_lowering=False)
    xTc = nc.dram_tensor("xTc", [128, KC, TPC], F32, kind="ExternalInput").ap()
    wc = nc.dram_tensor("wc", [KC, 128, 8], F32, kind="ExternalInput").ap()
    gout = nc.dram_tensor("gout", [TPC, 8], F32, kind="ExternalOutput").ap()
    p = P()
    with ExitStack() as st:
        e = st.enter_context
        lt = e(nc.sbuf_tensor("lt", [128, KC, 128], F32))
        wb = e(nc.sbuf_tensor("wb", [128, KC, 8], F32))
        sc = e(nc.sbuf_tensor("sc", [128, 8], F32))
        m0 = e(nc.sbuf_tensor("m0", [128, 1], F32))
        ge = e(nc.sbuf_tensor("ge", [128, 8], F32))
        mk = e(nc.sbuf_tensor("mk", [128, 8], F32))
        gg = e(nc.sbuf_tensor("gg", [128, 8], F32))
        ps = e(nc.psum_tensor("ps", [128, 8], F32))
        sem = e(nc.semaphore("S"))
        p.dma(wb[:], wc.transpose([1, 0, 2]))
        for t in range(NT):
            r = slice(t * 128, (t + 1) * 128)
            p.dma(lt[:], xTc[:, :, r])
            for c in range(KC):
                p.t("matmul", ps[:], lt[:, c, :], wb[:, c, :], start=(c == 0), stop=(c == KC - 1))
            p.s("activation", sc[:], ps[:], AF.Sigmoid)
            p.v("tensor_reduce", m0[:], sc[:], AX, OP.max)
            p.v("tensor_scalar", ge[:], sc[:], m0[:], None, OP.is_ge)
            p.v("tensor_scalar_mul", ge[:], ge[:], 1e9)
            p.v("tensor_sub", mk[:], sc[:], ge[:])
            p.v("tensor_reduce", m0[:], mk[:], AX, OP.max)
            p.v("tensor_scalar", ge[:], sc[:], m0[:], None, OP.is_ge)
            p.v("tensor_mul", gg[:], sc[:], ge[:])
            p.dma(gout[r, :], gg[:])
        p.build(nc, sem)
    return nc


def build_ffn():
    nc = bass.Bass("TRN2", target_bir_lowering=False)
    xTc = nc.dram_tensor("xTc", [128, KC, TPC], F32, kind="ExternalInput").ap()
    w13 = nc.dram_tensor("w13", [KC, 128, D], F32, kind="ExternalInput").ap()
    w2 = nc.dram_tensor("w2", [8, 128, D], F32, kind="ExternalInput").ap()
    gcol = nc.dram_tensor("gcol", [TPC, 1], F32, kind="ExternalInput").ap()
    ident = nc.dram_tensor("ident", [128, 128], F32, kind="ExternalInput").ap()
    pout = nc.dram_tensor("pout", [TPC, D], F32, kind="ExternalOutput").ap()
    p = P()
    with ExitStack() as st:
        e = st.enter_context
        lt = e(nc.sbuf_tensor("lt", [128, KC, 128], F32))
        wb = e(nc.sbuf_tensor("wb", [128, D], F32))
        idt = e(nc.sbuf_tensor("idt", [128, 128], F32))
        gc = e(nc.sbuf_tensor("gc", [128, 1], F32))
        z1 = e(nc.sbuf_tensor("z1", [128, 1024], F32))
        z3 = e(nc.sbuf_tensor("z3", [128, 1024], F32))
        hh = e(nc.sbuf_tensor("hh", [128, 1024], F32))
        hT = e(nc.sbuf_tensor("hT", [128, 8, 128], F32))
        ot = e(nc.sbuf_tensor("ot", [128, D], F32))
        ps = e(nc.psum_tensor("ps", [128, D], F32))
        psT = e(nc.psum_tensor("psT", [128, 128], F32))
        sem = e(nc.semaphore("S"))
        p.dma(idt[:], ident[:, :])
        for t in range(NT):
            r = slice(t * 128, (t + 1) * 128)
            p.dma(lt[:], xTc[:, :, r])
            p.dma(gc[:], gcol[r, :])
            for c in range(KC):
                p.dma(wb[:], w13[c])
                for jg in range(4):
                    js = slice(jg * 512, jg * 512 + 512)
                    p.t("matmul", ps[:, js], lt[:, c, :], wb[:, js],
                        start=(c == 0), stop=(c == KC - 1))
            p.v("tensor_scalar", z1[:], ps[:, 0:1024], gc[:], None, OP.mult)
            p.v("tensor_scalar", z3[:], ps[:, 1024:2048], gc[:], None, OP.mult)
            p.s("activation", z1[:], z1[:], AF.Silu)
            p.v("tensor_mul", hh[:], z1[:], z3[:])
            for q in range(8):
                p.t("transpose", psT[:], hh[:, q * 128:(q + 1) * 128], idt[:])
                p.v("tensor_copy", hT[:, q, :], psT[:])
            for q in range(8):
                p.dma(wb[:], w2[q])
                for jg in range(4):
                    js = slice(jg * 512, jg * 512 + 512)
                    p.t("matmul", ps[:, js], hT[:, q, :], wb[:, js],
                        start=(q == 0), stop=(q == 7))
            p.v("tensor_copy", ot[:], ps[:])
            p.dma(pout[r, :], ot[:])
        p.build(nc, sem)
    return nc


def build_sum(nparts):
    nc = bass.Bass("TRN2", target_bir_lowering=False)
    x1 = nc.dram_tensor("x1", [TPC, D], F32, kind="ExternalInput").ap()
    parts = [nc.dram_tensor("p%d" % i, [TPC, D], F32, kind="ExternalInput").ap()
             for i in range(nparts)]
    yout = nc.dram_tensor("yout", [TPC, D], F32, kind="ExternalOutput").ap()
    p = P()
    with ExitStack() as st:
        e = st.enter_context
        acc = e(nc.sbuf_tensor("acc", [128, D], F32))
        tb = e(nc.sbuf_tensor("tb", [128, D], F32))
        sem = e(nc.semaphore("S"))
        for t in range(NT):
            r = slice(t * 128, (t + 1) * 128)
            p.dma(acc[:], x1[r, :])
            for pa in parts:
                p.dma(tb[:], pa[r, :])
                p.v("tensor_add", acc[:], acc[:], tb[:])
            p.dma(yout[r, :], acc[:])
        p.build(nc, sem)
    return nc


def build_scan():
    nc = bass.Bass("TRN2", target_bir_lowering=False)
    xbp = nc.dram_tensor("xbp", [16, CPL, 1024], F32, kind="ExternalInput").ap()
    xcp = nc.dram_tensor("xcp", [16, CPL, 1024], F32, kind="ExternalInput").ap()
    xap = nc.dram_tensor("xap", [16, CPL, 1024], F32, kind="ExternalInput").ap()
    fmb = nc.dram_tensor("fmb", [64, CPL, 16, 16], F32, kind="ExternalInput").ap()
    fmc = nc.dram_tensor("fmc", [64, CPL, 16, 16], F32, kind="ExternalInput").ap()
    ilp = nc.dram_tensor("ilp", [16, CPL, 16], F32, kind="ExternalInput").ap()
    w1i = nc.dram_tensor("w1i", [64, 1024], F32, kind="ExternalInput").ap()
    b1i = nc.dram_tensor("b1i", [1, 1024], F32, kind="ExternalInput").ap()
    gmp = nc.dram_tensor("gmp", [16, 1024], F32, kind="ExternalInput").ap()
    btp = nc.dram_tensor("btp", [16, 1024], F32, kind="ExternalInput").ap()
    nmt = nc.dram_tensor("nmt", [16, 16], F32, kind="ExternalInput").ap()
    tir = nc.dram_tensor("tir", [16, 1], F32, kind="ExternalInput").ap()
    o1x16 = nc.dram_tensor("o1x16", [1, 16], F32, kind="ExternalInput").ap()
    no16 = nc.dram_tensor("no16", [16, 1], F32, kind="ExternalInput").ap()
    outp = nc.dram_tensor("outp", [16, CPL, 1024], F32, kind="ExternalOutput").ap()
    w1o = nc.dram_tensor("w1o", [64, 1024], F32, kind="ExternalOutput").ap()
    b1o = nc.dram_tensor("b1o", [1, 1024], F32, kind="ExternalOutput").ap()
    p = P()
    with ExitStack() as st:
        e = st.enter_context
        xb = e(nc.sbuf_tensor("xb", [16, CPL, 1024], F32))
        xc = e(nc.sbuf_tensor("xc", [16, CPL, 1024], F32))
        xa = e(nc.sbuf_tensor("xa", [16, CPL, 1024], F32))
        fb = e(nc.sbuf_tensor("fb", [64, CPL, 16, 16], F32))
        fc = e(nc.sbuf_tensor("fc", [64, CPL, 16, 16], F32))
        il = e(nc.sbuf_tensor("il", [16, CPL, 16], F32))
        w1 = e(nc.sbuf_tensor("w1", [64, 1024], F32))
        b1 = e(nc.sbuf_tensor("b1", [1, 1024], F32))
        gm = e(nc.sbuf_tensor("gm", [16, 1024], F32))
        bt = e(nc.sbuf_tensor("bt", [16, 1024], F32))
        nm = e(nc.sbuf_tensor("nm", [16, 16], F32))
        ti = e(nc.sbuf_tensor("ti", [16, 1], F32))
        o1 = e(nc.sbuf_tensor("o1", [1, 16], F32))
        no = e(nc.sbuf_tensor("no", [16, 1], F32))
        op = e(nc.sbuf_tensor("op", [16, CPL, 1024], F32))
        co = e(nc.sbuf_tensor("co", [16, 1], F32))
        nco = e(nc.sbuf_tensor("nco", [16, 1], F32))
        a1 = e(nc.sbuf_tensor("a1", [16, 16], F32))
        etn = e(nc.sbuf_tensor("etn", [16, 16], F32))
        mtn = e(nc.sbuf_tensor("mtn", [16, 16], F32))
        z1 = e(nc.sbuf_tensor("z1", [16, 64], F32))
        tgt = e(nc.sbuf_tensor("tgt", [16, 64], F32))
        s1 = e(nc.sbuf_tensor("s1", [16, 1], F32))
        s2 = e(nc.sbuf_tensor("s2", [16, 1], F32))
        rstd = e(nc.sbuf_tensor("rstd", [16, 1], F32))
        xce = e(nc.sbuf_tensor("xce", [16, 64], F32))
        sq = e(nc.sbuf_tensor("sq", [16, 64], F32))
        xh = e(nc.sbuf_tensor("xh", [16, 64], F32))
        yy = e(nc.sbuf_tensor("yy", [16, 64], F32))
        gg = e(nc.sbuf_tensor("gg", [16, 64], F32))
        gx = e(nc.sbuf_tensor("gx", [16, 64], F32))
        t1 = e(nc.sbuf_tensor("t1", [16, 64], F32))
        t2 = e(nc.sbuf_tensor("t2", [16, 64], F32))
        z1b = e(nc.sbuf_tensor("z1b", [16, 64], F32))
        cbn = e(nc.sbuf_tensor("cbn", [16, 64], F32))
        cg = e(nc.sbuf_tensor("cg", [16, 64], F32))
        oo = e(nc.sbuf_tensor("oo", [16, 64], F32))
        ps1 = e(nc.psum_tensor("ps1", [64, 64], F32))
        ps2 = e(nc.psum_tensor("ps2", [16, 16], F32))
        ps3 = e(nc.psum_tensor("ps3", [64, 64], F32))
        ps4 = e(nc.psum_tensor("ps4", [1, 64], F32))
        sem = e(nc.semaphore("S"))
        for (sb, dr) in ((xb, xbp), (xc, xcp), (xa, xap), (fb, fmb), (fc, fmc),
                         (il, ilp), (w1, w1i), (b1, b1i), (gm, gmp), (bt, btp),
                         (nm, nmt), (ti, tir), (o1, o1x16), (no, no16)):
            p.dma(sb[:], dr[:])
        for j in range(CPL):
            for i in range(16):
                hs = slice(64 * i, 64 * i + 64)
                xbv = xb[:, j, hs]; xcv = xc[:, j, hs]; xav = xa[:, j, hs]
                xbT = fb[:, j, i, :]; xcT = fc[:, j, i, :]
                w1h = w1[:, hs]; b1h = b1[:, hs]
                gam = gm[:, hs]; bet = bt[:, hs]
                p.v("tensor_mul", co[:], il[:, j, i:i + 1], ti[:])
                p.v("tensor_scalar_mul", nco[:], co[:], -1.0)
                p.t("matmul", ps1[0:16, :], o1[:], b1h, start=True, stop=False)
                p.t("matmul", ps1[0:16, :], xbT, w1h, start=False, stop=True)
                p.t("matmul", ps2[:], xbT, xcT, start=True, stop=True)
                p.v("tensor_copy", z1[:], ps1[0:16, :])
                p.v("tensor_sub", tgt[:], xav, xbv)
                p.v("tensor_reduce", s1[:], z1[:], AX, OP.add)
                p.v("tensor_scalar_mul", s1[:], s1[:], 1.0 / HD)
                p.v("tensor_scalar", xce[:], z1[:], s1[:], None, OP.subtract)
                p.v("tensor_mul", sq[:], xce[:], xce[:])
                p.v("tensor_reduce", s2[:], sq[:], AX, OP.add)
                p.v("tensor_scalar", s2[:], s2[:], 1.0 / HD, 1e-6, OP.mult, OP.add)
                p.s("sqrt", rstd[:], s2[:])
                p.v("reciprocal", rstd[:], rstd[:])
                p.v("tensor_scalar", xh[:], xce[:], rstd[:], None, OP.mult)
                p.v("tensor_mul", yy[:], xh[:], gam)
                p.v("tensor_add", yy[:], yy[:], bet)
                p.v("tensor_sub", gg[:], yy[:], tgt[:])
                p.v("tensor_mul", gg[:], gg[:], gam)
                p.v("tensor_reduce", s1[:], gg[:], AX, OP.add)
                p.v("tensor_mul", gx[:], gg[:], xh[:])
                p.v("tensor_reduce", s2[:], gx[:], AX, OP.add)
                p.v("tensor_scalar", t1[:], gg[:], 64.0, s1[:], OP.mult, OP.subtract)
                p.v("tensor_scalar", t2[:], xh[:], s2[:], None, OP.mult)
                p.v("tensor_sub", t1[:], t1[:], t2[:])
                p.v("tensor_scalar_mul", s1[:], rstd[:], 1.0 / HD)
                p.v("tensor_scalar", gg[:], t1[:], s1[:], None, OP.mult)
                p.v("tensor_scalar_add", a1[:], ps2[:], 1.0)
                p.v("tensor_scalar", etn[:], nm[:], co[:], None, OP.mult)
                p.v("tensor_mul", mtn[:], etn[:], a1[:])
                p.t("matmul", ps1[32:48, :], o1[:], b1h, start=True, stop=False)
                p.t("matmul", ps1[32:48, :], xcT, w1h, start=False, stop=False)
                p.t("matmul", ps1[32:48, :], mtn[:], gg[:], start=False, stop=True)
                p.v("tensor_copy", z1b[:], ps1[32:48, :])
                p.v("tensor_reduce", s1[:], z1b[:], AX, OP.add)
                p.v("tensor_scalar_mul", s1[:], s1[:], 1.0 / HD)
                p.v("tensor_scalar", xce[:], z1b[:], s1[:], None, OP.subtract)
                p.v("tensor_mul", sq[:], xce[:], xce[:])
                p.v("tensor_reduce", s2[:], sq[:], AX, OP.add)
                p.v("tensor_scalar", s2[:], s2[:], 1.0 / HD, 1e-6, OP.mult, OP.add)
                p.s("sqrt", rstd[:], s2[:])
                p.v("reciprocal", rstd[:], rstd[:])
                p.v("tensor_scalar", xh[:], xce[:], rstd[:], None, OP.mult)
                p.v("tensor_mul", oo[:], xh[:], gam)
                p.v("tensor_add", oo[:], oo[:], bet)
                p.v("tensor_add", op[:, j, hs], oo[:], xcv)
                p.v("tensor_scalar", cbn[:], xbv, nco[:], None, OP.mult)
                p.t("matmul", ps3[:], cbn[:], gg[:], start=True, stop=True)
                p.v("tensor_add", w1h, w1h, ps3[:])
                p.v("tensor_scalar", cg[:], gg[:], co[:], None, OP.mult)
                p.t("matmul", ps4[:], no[:], cg[:], start=True, stop=True)
                p.v("tensor_add", b1h, b1h, ps4[0:1, :])
        p.dma(outp[:], op[:])
        p.dma(w1o[:], w1[:])
        p.dma(b1o[:], b1[:])
        p.build(nc, sem)
    return nc


_progs = {}


def _get(key, fn, *a):
    if key not in _progs:
        _progs[key] = fn(*a)
    return _progs[key]


def _chunkT(x2d):
    t, k = x2d.shape
    return np.ascontiguousarray(x2d.reshape(t, k // 128, 128).transpose(2, 1, 0))


def _wchunk(w2d):
    k, n = w2d.shape
    return np.ascontiguousarray(w2d.reshape(k // 128, 128, n))


def _run(nc, maps):
    res = run_bass_kernel_spmd(nc, maps, core_ids=list(range(NCORE)))
    return res.results


def kernel(**inputs):
    f = lambda k: np.asarray(inputs[k], np.float32)
    x = f("L0_attn_hidden_states").reshape(TOK, D)
    Wq, Wk, Wv, Wo = f("L0_attn_Wq"), f("L0_attn_Wk"), f("L0_attn_Wv"), f("L0_attn_Wo")
    lw = f("L0_attn_linear_weight")[:, 0, :]
    lb = f("L0_attn_linear_bias")[:, 0]
    lnw, lnb = f("L0_attn_ln_weight"), f("L0_attn_ln_bias")
    W1i, b1i = f("L0_attn_W1_init"), f("L0_attn_b1_init")
    gw = f("L0_ffn_gate_w")
    w1, w2, w3 = f("L0_ffn_w1"), f("L0_ffn_w2"), f("L0_ffn_w3")
    ws1, ws2, ws3 = f("L0_ffn_ws1"), f("L0_ffn_ws2"), f("L0_ffn_ws3")

    ncLN = _get("ln", build_ln, 1e-5)
    ncMM = _get("mm", build_mm, False)
    ncMMr = _get("mmr", build_mm, True)
    ncILR = _get("ilr", build_ilr)
    ncGATE = _get("gate", build_gate)
    ncFFN = _get("ffn", build_ffn)
    ncSUM = _get("sum", build_sum, 9)
    ncSCAN = _get("scan", build_scan)

    def tok_slices(arr):
        return [np.ascontiguousarray(arr[c * TPC:(c + 1) * TPC]) for c in range(NCORE)]

    r = _run(ncLN, [{"xin": s} for s in tok_slices(x)])
    xln = np.concatenate([rr["xout"] for rr in r], 0)

    xlnT = [_chunkT(s) for s in tok_slices(xln)]
    proj = {}
    for nm, W in (("XC", Wq), ("XB", Wk), ("XA", Wv)):
        wcc = _wchunk(np.ascontiguousarray(W.T))
        rr = _run(ncMM, [{"xTc": xlnT[c], "wc": wcc} for c in range(NCORE)])
        proj[nm] = np.concatenate([q["yout"] for q in rr], 0)
    lbt = np.ascontiguousarray(np.tile(lb[None, :], (128, 1)))
    rr = _run(ncILR, [{"xTc": xlnT[c], "wc": _wchunk(np.ascontiguousarray(lw.T)),
                       "lbt": lbt} for c in range(NCORE)])
    ilr_sig = np.concatenate([q["yout"] for q in rr], 0)

    NEGMASKT = np.ascontiguousarray(-np.triu(np.ones((16, 16), np.float32)))
    TIR = (1.0 / (64.0 * np.arange(1, 17, dtype=np.float32))).reshape(16, 1)
    O1 = np.ones((1, 16), np.float32)
    NO16 = np.full((16, 1), -1.0, np.float32)
    core_meta = []
    for c in range(NCORE):
        b, hh = c // 2, c % 2
        rows = slice(b * 4096, (b + 1) * 4096)
        cols = slice(1024 * hh, 1024 * hh + 1024)
        hsl = slice(16 * hh, 16 * hh + 16)
        tm = {}
        fm = {}
        for nm in ("XB", "XC", "XA"):
            blk = proj[nm][rows, cols]
            tm[nm] = np.ascontiguousarray(
                blk.reshape(NCH, 16, 1024).transpose(1, 0, 2))
            fm[nm] = np.ascontiguousarray(
                blk.reshape(NCH, 16, 16, 64).transpose(3, 0, 2, 1))
        ilc = np.ascontiguousarray(
            ilr_sig[rows, hsl].reshape(NCH, 16, 16).transpose(1, 0, 2))
        w1s = np.ascontiguousarray(W1i[hsl].transpose(1, 0, 2).reshape(64, 1024))
        b1s = np.ascontiguousarray(b1i[hsl, 0, :].reshape(1, 1024))
        gmp = np.ascontiguousarray(np.tile(lnw[hsl].reshape(1, 1024), (16, 1)))
        btp = np.ascontiguousarray(np.tile(lnb[hsl].reshape(1, 1024), (16, 1)))
        core_meta.append(dict(tm=tm, fm=fm, il=ilc, w1=w1s, b1=b1s, gm=gmp, bt=btp))
    outs = [[] for _ in range(NCORE)]
    for n in range(NCH // CPL):
        js = slice(n * CPL, n * CPL + CPL)
        maps = []
        for c in range(NCORE):
            m = core_meta[c]
            maps.append({
                "xbp": np.ascontiguousarray(m["tm"]["XB"][:, js]),
                "xcp": np.ascontiguousarray(m["tm"]["XC"][:, js]),
                "xap": np.ascontiguousarray(m["tm"]["XA"][:, js]),
                "fmb": np.ascontiguousarray(m["fm"]["XB"][:, js]),
                "fmc": np.ascontiguousarray(m["fm"]["XC"][:, js]),
                "ilp": np.ascontiguousarray(m["il"][:, js]),
                "w1i": m["w1"], "b1i": m["b1"],
                "gmp": m["gm"], "btp": m["bt"],
                "nmt": NEGMASKT, "tir": TIR, "o1x16": O1, "no16": NO16,
            })
        rr = _run(ncSCAN, maps)
        for c in range(NCORE):
            core_meta[c]["w1"] = np.ascontiguousarray(rr[c]["w1o"])
            core_meta[c]["b1"] = np.ascontiguousarray(rr[c]["b1o"])
            outs[c].append(rr[c]["outp"])
    XCW = np.empty((TOK, D), np.float32)
    for c in range(NCORE):
        b, hh = c // 2, c % 2
        arr = np.stack(outs[c], 0)
        arr = arr.transpose(0, 2, 1, 3).reshape(4096, 1024)
        XCW[b * 4096:(b + 1) * 4096, 1024 * hh:1024 * hh + 1024] = arr

    woc = _wchunk(np.ascontiguousarray(Wo.T))
    rr = _run(ncMMr, [{"xTc": _chunkT(s), "wc": woc, "res": xs}
                      for s, xs in zip(tok_slices(XCW), tok_slices(x))])
    x1 = np.concatenate([q["yout"] for q in rr], 0)

    rr = _run(ncLN, [{"xin": s} for s in tok_slices(x1)])
    x1ln = np.concatenate([q["xout"] for q in rr], 0)
    x1lnT = [_chunkT(s) for s in tok_slices(x1ln)]
    rr = _run(ncGATE, [{"xTc": x1lnT[c], "wc": _wchunk(np.ascontiguousarray(gw.T))}
                       for c in range(NCORE)])
    g = np.concatenate([q["gout"] for q in rr], 0)
    IDENT = np.eye(128, dtype=np.float32)
    partials = []
    for e in range(8):
        w13 = _wchunk(np.ascontiguousarray(
            np.concatenate([w1[e].T, w3[e].T], axis=1)))
        w2c = _wchunk(np.ascontiguousarray(w2[e].T))
        gcol = np.ascontiguousarray(g[:, e:e + 1])
        rr = _run(ncFFN, [{"xTc": x1lnT[c], "w13": w13, "w2": w2c,
                           "gcol": np.ascontiguousarray(gcol[c * TPC:(c + 1) * TPC]),
                           "ident": IDENT} for c in range(NCORE)])
        partials.append([q["pout"] for q in rr])
    w13s = _wchunk(np.ascontiguousarray(np.concatenate([ws1.T, ws3.T], axis=1)))
    w2s = _wchunk(np.ascontiguousarray(ws2.T))
    ones_g = np.ones((TPC, 1), np.float32)
    rr = _run(ncFFN, [{"xTc": x1lnT[c], "w13": w13s, "w2": w2s,
                       "gcol": ones_g, "ident": IDENT} for c in range(NCORE)])
    partials.append([q["pout"] for q in rr])
    maps = []
    for c in range(NCORE):
        m = {"x1": tok_slices(x1)[c]}
        for i in range(9):
            m["p%d" % i] = np.ascontiguousarray(partials[i][c])
        maps.append(m)
    rr = _run(ncSUM, maps)
    out = np.concatenate([q["yout"] for q in rr], 0)
    return out.reshape(4, 4096, 2048).astype(np.float32)
